# version 01 (final)
# baseline (speedup 1.0000x reference)
import math
import numpy as np
import ml_dtypes
import concourse.bass as bass
import concourse.mybir as mybir
from concourse.bass_utils import run_bass_kernel_spmd
from contextlib import ExitStack

F32 = mybir.dt.float32
BF16 = mybir.dt.bfloat16
AF = mybir.ActivationFunctionType
OP = mybir.AluOpType
BF = ml_dtypes.bfloat16

D, T, B, NCORE = 4096, 4096, 2, 8
TQ = 512
NTI = T // TQ
DFF = 11008
THIRDS = [29, 29, 28]
SCALE = 1.0 / math.sqrt(128.0)
CSHIFT = 12.0
EPS = 1e-5
RG = [[0, 1, 2, 3], [4, 5, 6, 7]]
ENGS = ("tensor", "scalar", "vector", "gpsimd", "sync")


class Sem:
    __slots__ = ("h", "v", "name")

    def __init__(self, h, name):
        self.h, self.v, self.name = h, 0, name


class Builder:
    def __init__(self, dumps=(), use_switch=True):
        self.nc = bass.Bass("TRN2", target_bir_lowering=False)
        self.plan = {e: [] for e in ENGS}
        self.dumps = dumps
        self.use_switch = use_switch
        self.dump_aps = {}

    def sem(self, name):
        return Sem(self.es.enter_context(self.nc.semaphore("s_" + name)), name)

    def emit(self, eng, fn, wait=(), inc=None, incn=1):
        wait = tuple((s.h, v) for (s, v) in wait if v > 0)
        ih = None
        if inc is not None:
            ih = inc.h
            inc.v += incn

        def cl(e, fn=fn, wait=wait, ih=ih, incn=incn):
            for h, v in wait:
                e.wait_ge(h, v)
            inst = fn(e)
            if ih is not None:
                inst.then_inc(ih, incn)

        self.plan[eng].append(cl)

    def dma(self, eng, out, in_, wait=(), inc=None):
        self.emit(eng, lambda e: e.dma_start(out=out, in_=in_), wait=wait, inc=inc, incn=16)

    def dump(self, name, shape, dtype):
        if name in self.dumps and name not in self.dump_aps:
            self.dump_aps[name] = self.nc.dram_tensor("dbg_" + name, shape, dtype,
                                                      kind="ExternalOutput").ap()
        return self.dump_aps.get(name)

    def dump_dram(self, name, src, shape, dtype, tmp):
        dst = self.dump(name, shape, dtype)
        if dst is None:
            return
        sm = self.sm
        rows, cols = shape
        for r0 in range(0, rows, 128):
            rn = min(128, rows - r0)
            self.dma("sync", tmp[0:rn, 0:cols], src[r0:r0 + rn, :],
                     wait=[(sm["dbg"], sm["dbg"].v)], inc=sm["dbg"])
            self.dma("sync", dst[r0:r0 + rn, :], tmp[0:rn, 0:cols],
                     wait=[(sm["dbg"], sm["dbg"].v)], inc=sm["dbg"])

    def build(self, phases=9):
        nc = self.nc
        P = nc.dram_tensor
        self.xT = P("xT", [D, T], F32, kind="ExternalInput").ap()
        self.xqi = P("xq", [D, 1024], F32, kind="ExternalInput").ap()
        self.w1t = P("w1t", [D, 1536], F32, kind="ExternalInput").ap()
        self.b1v = P("b1v", [1536], F32, kind="ExternalInput").ap()
        self.w2t = P("w2t", [D, D], F32, kind="ExternalInput").ap()
        self.b2v = P("b2v", [D], F32, kind="ExternalInput").ap()
        self.wgt = P("wgt", [D, DFF], F32, kind="ExternalInput").ap()
        self.wut = P("wut", [D, DFF], F32, kind="ExternalInput").ap()
        self.wdt = P("wdt", [DFF, D], F32, kind="ExternalInput").ap()
        self.cE = P("cE", [128, 512], BF16, kind="ExternalInput").ap()
        self.cF = P("cF", [128, 512], BF16, kind="ExternalInput").ap()
        self.cI = P("cI", [128, 128], BF16, kind="ExternalInput").ap()
        self.cM = P("cM", [128, 2048], BF16, kind="ExternalInput").ap()
        self.cOc = P("cOc", [128, 1], BF16, kind="ExternalInput").ap()
        self.cOr = P("cOr", [1, 128], BF16, kind="ExternalInput").ap()
        self.cIND = P("cIND", [32, 1024], BF16, kind="ExternalInput").ap()
        self.out = P("out", [D, 1024], F32, kind="ExternalOutput").ap()
        self.qd = P("qd", [1024, T], BF16, kind="Internal").ap()
        self.ctb = P("ctb", [264, T], BF16, kind="Internal").ap()
        self.gth = P("gth", [1056, T], BF16, kind="Internal").ap()
        self.h2d = P("h2d", [D, 1024], BF16, kind="Internal").ap()
        self.x1d = P("x1d", [D, 1024], F32, kind="Internal").ap()
        self.ypd = P("ypd", [D, 1024], F32, kind="Internal").ap()

        self.es = ExitStack()
        with self.es:
            S = self.es.enter_context
            self.identsb = S(nc.sbuf_tensor("identsb", [128, 128], BF16))
            self.Esb = S(nc.sbuf_tensor("Esb", [128, 512], BF16))
            self.Fsb = S(nc.sbuf_tensor("Fsb", [128, 512], BF16))
            self.masksb = S(nc.sbuf_tensor("masksb", [128, 2048], BF16))
            self.onescol = S(nc.sbuf_tensor("onescol", [128, 1], BF16))
            self.onesrow = S(nc.sbuf_tensor("onesrow", [1, 128], BF16))
            self.indsb = S(nc.sbuf_tensor("indsb", [32, 1024], BF16))
            self.b1c = S(nc.sbuf_tensor("b1c", [128, 12], F32))
            self.b2c = S(nc.sbuf_tensor("b2c", [128, 32], F32))
            self.cshift = S(nc.sbuf_tensor("cshift", [128, 1], F32))
            self.row_a = S(nc.sbuf_tensor("row_a", [1, TQ], F32))
            self.row_b = S(nc.sbuf_tensor("row_b", [1, TQ], F32))
            self.row_c = S(nc.sbuf_tensor("row_c", [1, TQ], BF16))
            self.row_d = S(nc.sbuf_tensor("row_d", [1, TQ], BF16))
            self.psA = S(nc.psum_tensor("psA", [128, 1024], F32))
            self.psB = S(nc.psum_tensor("psB", [128, 1024], F32))
            self.psav = S(nc.psum_tensor("psav", [128, 512], F32))
            self.psx1 = S(nc.psum_tensor("psx1", [128, 512], F32))
            self.psx2 = S(nc.psum_tensor("psx2", [128, 512], F32))
            self.pstp = S(nc.psum_tensor("pstp", [128, 1024], BF16))
            self.bank4 = [self.psA[:, 0:512], self.psA[:, 512:1024],
                          self.psB[:, 0:512], self.psB[:, 512:1024]]

            names = ("c0 xld0 xld1 w1ld sq stm rows rbc w1s qmm qep_d qep_a qst "
                     "vtp vtc krepm krep qld0 qld1 qk exp msk avm oden ost "
                     "w2ld fold foldev cc grd rden indm onrm orhsn "
                     "w2mm xqld w2ep_a w2ep_d l2st l2rows l2bc h2dv h2st x1st "
                     "gwld0 gwld1 h2ld gumm silu hffm wdld0 wdld1 dnmm ypld x1ld dnev outst "
                     "dbg").split()
            self.sm = {n: self.sem(n) for n in names}

            self.es_kv = ExitStack()
            self.es_qkv = ExitStack()
            self.es_att = ExitStack()
            self.es_w2 = ExitStack()
            self.es_p4 = ExitStack()

            self.p0_consts()
            self.p1_qkv()
            if phases >= 2:
                self.p2_attention()
            if phases >= 3:
                self.p3_collective()
            if phases >= 4:
                self.p4_w2_ln2()
            if phases >= 5:
                self.p5_ffn()

            for st in (self.es_p4, self.es_w2, self.es_att, self.es_qkv, self.es_kv):
                st.close()
            sm = self.sm
            fin = tuple((s.h, s.v) for s in (sm["outst"], sm["dbg"], sm["qst"], sm["ost"],
                                             sm["x1st"], sm["h2st"]) if s.v > 0)

            def final_wait(e, fin=fin):
                for h, v in fin:
                    e.wait_ge(h, v)
            self.plan["sync"].append(final_wait)

            block = S(nc.Block())
            for name in ENGS:
                lst = self.plan[name]

                def runner(e, lst=lst):
                    for cl in lst:
                        cl(e)
                getattr(block, name)(runner)
        return nc

    def p0_consts(self):
        sm = self.sm
        for dst, src in ((self.identsb, self.cI), (self.Esb, self.cE), (self.Fsb, self.cF),
                         (self.masksb, self.cM), (self.onescol, self.cOc),
                         (self.onesrow, self.cOr), (self.indsb, self.cIND)):
            self.dma("sync", dst[:], src, inc=sm["c0"])
        for oc in range(12):
            self.dma("sync", self.b1c[:, oc:oc + 1],
                     self.b1v[128 * oc:128 * (oc + 1)].rearrange("(p o) -> p o", o=1),
                     inc=sm["c0"])
        for oc in range(32):
            self.dma("sync", self.b2c[:, oc:oc + 1],
                     self.b2v[128 * oc:128 * (oc + 1)].rearrange("(p o) -> p o", o=1),
                     inc=sm["c0"])
        self.emit("vector", lambda e: e.memset(self.cshift[:], -CSHIFT), inc=sm["c0"])
        self.c0_done = sm["c0"].v

    def p1_qkv(self):
        nc, sm = self.nc, self.sm
        kv = self.es_kv.enter_context
        self.ksb = [kv(nc.sbuf_tensor(f"k{i}", [128, T], BF16)) for i in range(2)]
        self.vsb = [kv(nc.sbuf_tensor(f"v{i}", [128, T], BF16)) for i in range(2)]
        eq = self.es_qkv.enter_context
        w1sb = eq(nc.sbuf_tensor("w1sb", [128, 4 * 32 * 128], BF16))
        xb = [eq(nc.sbuf_tensor(f"xb{i}", [128, 32 * TQ], BF16)) for i in range(2)]
        x2sl = [eq(nc.sbuf_tensor(f"x2sl{i}", [128, 4096], BF16)) for i in range(2)]
        qstage = [eq(nc.sbuf_tensor(f"qstage{i}", [128, TQ], BF16)) for i in range(2)]
        qkv1 = [eq(nc.sbuf_tensor(f"qkv1_{i}", [128, TQ], F32)) for i in range(2)]
        rbcast = eq(nc.sbuf_tensor("rbcast", [128, T], BF16))
        negm_bf = eq(nc.sbuf_tensor("negm_bf", [1, T], BF16))
        row_e = eq(nc.sbuf_tensor("row_e", [1, TQ], F32))
        w1s_bf = eq(nc.sbuf_tensor("w1s_bf", [1, 1536], BF16))

        xcons = [0, 0]
        site = 0
        qep_d_hist = []
        qep_a_hist = []
        qst_hist = []
        nqst = 0

        for p in range(3):
            ocs = [4 * p + i for i in range(4)]
            w1_wait = [(sm["qmm"], sm["qmm"].v)] if p > 0 else []
            for i, oc in enumerate(ocs):
                self.dma("gpsimd",
                         w1sb[:].rearrange("p (ic o) -> p ic o", o=128)[:, 32 * i:32 * (i + 1), :],
                         self.w1t[:, 128 * oc:128 * (oc + 1)].rearrange("(c p) o -> p c o", p=128),
                         wait=w1_wait if i == 0 else [], inc=sm["w1ld"])
            w1_ready = sm["w1ld"].v

            for i, oc in enumerate(ocs):
                for c in range(32):
                    w = []
                    if c == 0:
                        w = [(sm["w1s"], sm["w1s"].v)]
                        if i == 0:
                            w += [(sm["w1ld"], w1_ready), (sm["c0"], self.c0_done),
                                  (sm["rows"], sm["rows"].v), (sm["rbc"], sm["rbc"].v)]
                    self.emit("tensor",
                              lambda e, i=i, c=c: e.matmul(
                                  self.psav[0:1, 0:128], self.onescol[:],
                                  w1sb[:, (i * 32 + c) * 128:(i * 32 + c + 1) * 128],
                                  start=(c == 0), stop=(c == 31)),
                              wait=w, inc=(sm["qmm"] if c == 31 else None))
                self.emit("vector",
                          lambda e, oc=oc: e.tensor_copy(
                              w1s_bf[0:1, 128 * oc:128 * (oc + 1)], self.psav[0:1, 0:128]),
                          wait=[(sm["qmm"], sm["qmm"].v)], inc=sm["w1s"])
            w1s_evac = sm["w1s"].v

            for ti in range(NTI):
                par = ti % 2
                xsem = sm["xld0"] if par == 0 else sm["xld1"]
                xw = [(sm["qmm"], xcons[par])] if xcons[par] > 0 else []
                self.dma("gpsimd", xb[par][:].rearrange("p (c f) -> p c f", f=TQ),
                         self.xT[:, TQ * ti:TQ * (ti + 1)].rearrange("(c p) f -> p c f", p=128),
                         wait=xw, inc=xsem)
                x_ready = xsem.v
                tis = slice(TQ * ti, TQ * (ti + 1))

                if p == 0:
                    slab_hist = getattr(self, "_slab_hist", [])
                    for s4 in range(4):
                        slab = x2sl[s4 % 2]
                        sqw = [(xsem, x_ready)]
                        if len(slab_hist) >= 2:
                            sqw.append((sm["stm"], slab_hist[-2]))
                        self.emit("scalar",
                                  lambda e, slab=slab, s4=s4, par=par: e.activation(
                                      out=slab[:], in_=xb[par][:, 4096 * s4:4096 * (s4 + 1)],
                                      func=AF.Square, bias=0.0, scale=1.0),
                                  wait=sqw, inc=sm["sq"])
                        sq_ready = sm["sq"].v
                        for c8 in range(8):
                            c = 8 * s4 + c8
                            w1c = []
                            if c == 0:
                                w1c = [(xsem, x_ready), (sm["rows"], sm["rows"].v),
                                       (sm["w1s"], w1s_evac)]
                            self.emit("tensor",
                                      lambda e, c=c, par=par: e.matmul(
                                          self.psav[0:1, 0:512], self.onescol[:],
                                          xb[par][:, TQ * c:TQ * (c + 1)],
                                          start=(c == 0), stop=(c == 31)),
                                      wait=w1c)
                            w2c = []
                            if c8 == 0:
                                w2c = [(sm["sq"], sq_ready)]
                                if c == 0:
                                    w2c.append((sm["rows"], sm["rows"].v))
                            self.emit("tensor",
                                      lambda e, c8=c8, slab=slab, c=c: e.matmul(
                                          self.psx1[0:1, 0:512], self.onescol[:],
                                          slab[:, TQ * c8:TQ * (c8 + 1)],
                                          start=(c == 0), stop=(c == 31)),
                                      wait=w2c,
                                      inc=(sm["stm"] if c8 == 7 else None))
                        slab_hist.append(sm["stm"].v)
                    self._slab_hist = slab_hist
                    stats_done = sm["stm"].v
                    self.emit("vector", lambda e: e.tensor_scalar(
                        row_e[:], self.psav[0:1, 0:512], -1.0 / D, None, OP.mult),
                        wait=[(sm["stm"], stats_done)])
                    self.emit("vector", lambda e: e.tensor_tensor(
                        self.row_a[:], row_e[:], row_e[:], OP.mult))
                    self.emit("vector", lambda e: e.tensor_scalar(
                        self.row_b[:], self.psx1[0:1, 0:512], 1.0 / D, EPS, OP.mult, OP.add))
                    self.emit("vector", lambda e: e.tensor_tensor(
                        self.row_b[:], self.row_b[:], self.row_a[:], OP.subtract),
                        inc=sm["rows"])
                    dv_rows = sm["rows"].v
                    self.emit("scalar", lambda e: e.activation(
                        out=self.row_b[:], in_=self.row_b[:], func=AF.Ln, bias=0.0, scale=1.0),
                        wait=[(sm["rows"], dv_rows)])
                    self.emit("scalar", lambda e: e.activation(
                        out=self.row_a[:], in_=self.row_b[:], func=AF.Exp, bias=0.0, scale=-0.5),
                        inc=sm["rows"])
                    act_rows = sm["rows"].v
                    self.emit("vector", lambda e, tis=tis: e.tensor_copy(
                        negm_bf[0:1, tis], row_e[:]))
                    self.emit("vector", lambda e: e.tensor_copy(
                        self.row_d[:], self.row_a[:]),
                        wait=[(sm["rows"], act_rows)], inc=sm["rows"])
                    rows_done = sm["rows"].v
                    self.emit("tensor", lambda e: e.matmul(
                        self.psx2[:, 0:512], self.onesrow[:], self.row_d[:],
                        start=True, stop=True),
                        wait=[(sm["rows"], rows_done), (sm["rbc"], sm["rbc"].v)],
                        inc=sm["stm"])
                    self.emit("vector", lambda e, tis=tis: e.tensor_copy(
                        rbcast[:, tis], self.psx2[:, 0:512]),
                        wait=[(sm["stm"], sm["stm"].v)], inc=sm["rbc"])

                for i, oc in enumerate(ocs):
                    bank = self.bank4[site % 4]
                    mmw = []
                    if site >= 4:
                        mmw.append((sm["qep_d"], qep_d_hist[site - 4]))
                    if i == 0:
                        mmw += [(xsem, x_ready), (sm["rows"], sm["rows"].v),
                                (sm["w1s"], w1s_evac)]
                    for c in range(32):
                        self.emit("tensor",
                                  lambda e, i=i, c=c, bank=bank, par=par: e.matmul(
                                      bank, w1sb[:, (i * 32 + c) * 128:(i * 32 + c + 1) * 128],
                                      xb[par][:, TQ * c:TQ * (c + 1)],
                                      start=(c == 0), stop=False),
                                  wait=mmw if c == 0 else [])
                    self.emit("tensor",
                              lambda e, oc=oc, bank=bank, tis=tis: e.matmul(
                                  bank, w1s_bf[0:1, 128 * oc:128 * (oc + 1)],
                                  negm_bf[0:1, tis], start=False, stop=True),
                              inc=sm["qmm"])
                    mm_done = sm["qmm"].v
                    epar = site % 2
                    dvw = [(sm["qmm"], mm_done), (sm["rbc"], sm["rbc"].v)]
                    if site >= 2:
                        dvw.append((sm["qep_a"], qep_a_hist[site - 2]))
                    self.emit("vector",
                              lambda e, bank=bank, epar=epar, tis=tis: e.tensor_tensor(
                                  qkv1[epar][:], bank, rbcast[:, tis], OP.mult),
                              wait=dvw, inc=sm["qep_d"])
                    qep_d_hist.append(sm["qep_d"].v)
                    if oc < 8:
                        aw = [(sm["qep_d"], sm["qep_d"].v)]
                        if nqst >= 2:
                            aw.append((sm["qst"], qst_hist[nqst - 2]))
                        self.emit("scalar",
                                  lambda e, epar=epar, oc=oc: e.activation(
                                      out=qstage[epar][:], in_=qkv1[epar][:], func=AF.Identity,
                                      bias=self.b1c[:, oc:oc + 1], scale=1.0),
                                  wait=aw, inc=sm["qep_a"])
                        self.dma("sync", self.qd[128 * oc:128 * (oc + 1), tis], qstage[epar][:],
                                 wait=[(sm["qep_a"], sm["qep_a"].v)], inc=sm["qst"])
                        qst_hist.append(sm["qst"].v)
                        nqst += 1
                    else:
                        dst = {8: self.ksb[0], 9: self.ksb[1],
                               10: self.vsb[0], 11: self.vsb[1]}[oc]
                        self.emit("scalar",
                                  lambda e, dst=dst, epar=epar, oc=oc, tis=tis: e.activation(
                                      out=dst[:, tis], in_=qkv1[epar][:], func=AF.Identity,
                                      bias=self.b1c[:, oc:oc + 1], scale=1.0),
                                  wait=[(sm["qep_d"], sm["qep_d"].v)], inc=sm["qep_a"])
                    qep_a_hist.append(sm["qep_a"].v)
                    site += 1
                xcons[par] = sm["qmm"].v

        self.qkv_mm_done = sm["qmm"].v
        self.qkv_ep_done = sm["qep_a"].v
        self.qkv_dve_done = sm["qep_d"].v
        self.q_stored = sm["qst"].v

        if "kv" in self.dumps:
            dk = self.dump("kv", [512, T], BF16)
            for i in range(2):
                self.dma("sync", dk[128 * i:128 * (i + 1), :], self.ksb[i][:],
                         wait=[(sm["qep_a"], self.qkv_ep_done)], inc=sm["dbg"])
                self.dma("sync", dk[256 + 128 * i:256 + 128 * (i + 1), :], self.vsb[i][:],
                         wait=[(sm["qep_a"], self.qkv_ep_done)], inc=sm["dbg"])
        if "q" in self.dumps:
            dq = self.dump("q", [1024, T], BF16)
            for oc in range(8):
                self.dma("sync", qstage[0][:], self.qd[128 * oc:128 * (oc + 1), 0:TQ],
                         wait=[(sm["qst"], self.q_stored), (sm["dbg"], sm["dbg"].v)],
                         inc=sm["dbg"])
                self.dma("sync", dq[128 * oc:128 * (oc + 1), 0:TQ], qstage[0][:],
                         wait=[(sm["dbg"], sm["dbg"].v)], inc=sm["dbg"])

    def p2_attention(self):
        nc, sm = self.nc, self.sm
        self.es_qkv.close()
        ea = self.es_att.enter_context
        self.vext = self.es_kv.enter_context(nc.sbuf_tensor("vext", [128, 32 * 8 * 33], BF16))
        krep = [ea(nc.sbuf_tensor(f"krep{i}", [128, T], BF16)) for i in range(2)]
        qld = [ea(nc.sbuf_tensor(f"qld{i}", [128, TQ], BF16)) for i in range(2)]
        psb = [ea(nc.sbuf_tensor(f"psb{i}", [128, 1024], BF16)) for i in range(2)]
        ostg = [ea(nc.sbuf_tensor(f"ostg{i}", [32, TQ], BF16)) for i in range(2)]
        dstg = [ea(nc.sbuf_tensor(f"dstg{i}", [1, TQ], BF16)) for i in range(2)]
        w2sl = [ea(nc.sbuf_tensor(f"w2sl{i}", [128, 32 * 128], BF16)) for i in range(2)]
        self.w2ft = self.es_w2.enter_context(nc.sbuf_tensor("w2ft", [128, 8 * 4096], BF16, side="right"))
        vext = self.vext

        self.emit("vector", lambda e: e.memset(vext[:], 1.0),
                  wait=[(sm["qep_d"], self.qkv_dve_done)], inc=sm["vtc"])
        tp = 0
        for ti in range(32):
            for vc in range(2):
                half = self.pstp[:, 512 * (tp % 2):512 * (tp % 2) + 128]
                w = []
                if tp == 0:
                    w = [(sm["qep_a"], self.qkv_ep_done), (sm["c0"], self.c0_done)]
                if tp >= 1:
                    w = [(sm["vtc"], 1 + 4 * tp)]
                self.emit("tensor",
                          lambda e, half=half, vc=vc, ti=ti: e.transpose(
                              half, self.vsb[vc][:, 128 * ti:128 * (ti + 1)], self.identsb[:]),
                          wait=w, inc=sm["vtp"])
                vtp = sm["vtp"].v
                for hh in range(4):
                    hl = 4 * vc + hh
                    col = (ti * 8 + hl) * 33
                    self.emit("vector",
                              lambda e, half=half, hh=hh, col=col: e.tensor_copy(
                                  vext[:, col:col + 32], half[:, 32 * hh:32 * (hh + 1)]),
                              wait=[(sm["vtp"], vtp)] if hh == 0 else [], inc=sm["vtc"])
                tp += 1
        vext_done = sm["vtc"].v
        vtp_all = sm["vtp"].v

        qk_hist_site = []
        exp_hist = []
        avm_hist = []
        oden_hist = []
        ost_hist = []
        qk_after_head = []
        self.krep_jobs = []

        def plan_krep(h):
            slot = krep[h % 2]
            kc = self.ksb[h // 4]
            hp = h % 4
            prev_user_qk = qk_after_head[h - 2] if h >= 2 else 0
            for tc in range(8):
                def mmf(e, hp=hp, kc=kc, tc=tc):
                    return e.matmul(self.psx1[:, 0:512],
                                    self.Esb[:, 128 * hp:128 * (hp + 1)],
                                    kc[:, TQ * tc:TQ * (tc + 1)], start=True, stop=True)

                def evf(e, slot=slot, tc=tc):
                    return e.tensor_copy(slot[:, TQ * tc:TQ * (tc + 1)], self.psx1[:, 0:512])
                self.krep_jobs.append((mmf, evf, prev_user_qk if tc == 0 else 0, h))

        def emit_krep_job():
            if not self.krep_jobs:
                return
            mmf, evf, prev_qk, h = self.krep_jobs.pop(0)
            w = [(sm["krep"], sm["krep"].v)]
            if prev_qk:
                w.append((sm["qk"], prev_qk))
            if h <= 1:
                w.append((sm["qep_a"], self.qkv_ep_done))
            self.emit("tensor", mmf, wait=w, inc=sm["krepm"])
            self.emit("vector", evf, wait=[(sm["krepm"], sm["krepm"].v)], inc=sm["krep"])

        fold_state = {"sl": 0}
        foldev_hist = []

        def emit_fold_slice():
            sl = fold_state["sl"]
            if sl >= 32:
                return
            fold_state["sl"] += 1
            par = sl % 2
            w = [(sm["qmm"], self.qkv_mm_done)] if sl < 2 else \
                [(sm["fold"], 32 * (sl - 1))]
            self.dma("gpsimd", w2sl[par][:].rearrange("p (c o) -> p c o", o=128),
                     self.w2t[:, 128 * sl:128 * (sl + 1)].rearrange("(c p) o -> p c o", p=128),
                     wait=w, inc=sm["w2ld"])
            ready = sm["w2ld"].v
            for jjc in range(8):
                fw = [(sm["w2ld"], ready), (sm["foldev"], sm["foldev"].v)]
                for ci in range(4):
                    cc = 4 * jjc + ci
                    self.emit("tensor",
                              lambda e, ci=ci, cc=cc, par=par: e.matmul(
                                  self.psx2[:, 0:128], self.Fsb[:, 128 * ci:128 * (ci + 1)],
                                  w2sl[par][:, 128 * cc:128 * (cc + 1)],
                                  start=(ci == 0), stop=(ci == 3)),
                              wait=fw if ci == 0 else [], inc=sm["fold"])
                self.emit("vector",
                          lambda e, jjc=jjc, sl=sl: e.tensor_copy(
                              self.w2ft[:, 4096 * jjc + 128 * sl:4096 * jjc + 128 * (sl + 1)],
                              self.psx2[:, 0:128]),
                          wait=[(sm["fold"], sm["fold"].v)], inc=sm["foldev"])
            foldev_hist.append(sm["foldev"].v)

        plan_krep(0)
        for _ in range(8):
            emit_krep_job()
        gidx = 0
        qsite = 0
        for h in range(8):
            if h < 7:
                plan_krep(h + 1)
            krep_ready = 8 * (h + 1)
            for qi in range(NTI):
                qpar = qsite % 2
                qsem = sm["qld0"] if qpar == 0 else sm["qld1"]
                qw = [(sm["qst"], self.q_stored)]
                if qsite < 2:
                    qw.append((sm["qep_a"], self.qkv_ep_done))
                if qsite >= 2:
                    qw.append((sm["qk"], qk_hist_site[qsite - 2]))
                self.dma("sync", qld[qpar][:],
                         self.qd[128 * h:128 * (h + 1), TQ * qi:TQ * (qi + 1)],
                         wait=qw, inc=qsem)
                q_ready = qsem.v
                nki = 4 * qi + 4
                ngr = nki // 2
                for gi in range(ngr):
                    k0 = 2 * gi
                    spar = gidx % 2
                    sbank = self.psA if spar == 0 else self.psB
                    for j in range(2):
                        ki = k0 + j
                        w = []
                        if j == 0:
                            w = [(qsem, q_ready), (sm["krep"], krep_ready)]
                            if gidx >= 2:
                                w.append((sm["exp"], exp_hist[gidx - 2]))
                        self.emit("tensor",
                                  lambda e, sbank=sbank, j=j, ki=ki, slot=krep[h % 2],
                                  qpar=qpar: e.matmul(
                                      sbank[:, 512 * j:512 * (j + 1)],
                                      slot[:, 128 * ki:128 * (ki + 1)],
                                      qld[qpar][:], start=True, stop=True),
                                  wait=w, inc=(sm["qk"] if j == 1 else None))
                    qk_done = sm["qk"].v
                    ew = [(sm["qk"], qk_done)]
                    if gidx >= 2:
                        ew.append((sm["avm"], avm_hist[gidx - 2]))
                    self.emit("scalar",
                              lambda e, sbank=sbank, spar=spar: e.activation(
                                  out=psb[spar][:], in_=sbank[:],
                                  func=AF.Exp, bias=self.cshift[:], scale=SCALE),
                              wait=ew, inc=sm["exp"])
                    exp_hist.append(sm["exp"].v)
                    exp_done = sm["exp"].v
                    masked = False
                    for j in range(2):
                        r = k0 + j - 4 * qi
                        if r >= 0:
                            self.emit("vector",
                                      lambda e, spar=spar, j=j, r=r: e.tensor_tensor(
                                          psb[spar][:, 512 * j:512 * (j + 1)],
                                          psb[spar][:, 512 * j:512 * (j + 1)],
                                          self.masksb[:, 512 * r:512 * (r + 1)], OP.mult),
                                      wait=[(sm["exp"], exp_done)] if not masked else [],
                                      inc=sm["msk"])
                            masked = True
                    msk_done = sm["msk"].v
                    for j in range(2):
                        ki = k0 + j
                        col = (ki * 8 + h) * 33
                        w = []
                        if j == 0:
                            w = [(sm["msk"], msk_done)] if masked else [(sm["exp"], exp_done)]
                            if gi == 0:
                                if qsite >= 1:
                                    w.append((sm["oden"], oden_hist[qsite - 1]))
                                if qsite == 0:
                                    w.append((sm["vtc"], vext_done))
                        self.emit("tensor",
                                  lambda e, col=col, spar=spar, j=j, ki=ki, nki=nki: e.matmul(
                                      self.psav[0:33, 0:512], vext[:, col:col + 33],
                                      psb[spar][:, 512 * j:512 * (j + 1)],
                                      start=(ki == 0), stop=(ki == nki - 1)),
                                  wait=w, inc=(sm["avm"] if j == 1 else None))
                    avm_hist.append(sm["avm"].v)
                    gidx += 1
                    emit_krep_job()
                qk_hist_site.append(sm["qk"].v)
                av_done = sm["avm"].v
                opar = qsite % 2
                ow = [(sm["avm"], av_done)]
                if qsite >= 2:
                    ow.append((sm["ost"], ost_hist[qsite - 2]))
                self.emit("vector",
                          lambda e, opar=opar: e.tensor_copy(ostg[opar][:],
                                                             self.psav[0:32, 0:512]),
                          wait=ow, inc=sm["oden"])
                self.emit("vector",
                          lambda e, opar=opar: e.tensor_copy(dstg[opar][:],
                                                             self.psav[32:33, 0:512]),
                          inc=sm["oden"])
                oden_hist.append(sm["oden"].v)
                od = sm["oden"].v
                self.dma("sync", self.ctb[32 * h:32 * (h + 1), TQ * qi:TQ * (qi + 1)],
                         ostg[opar][:], wait=[(sm["oden"], od)], inc=sm["ost"])
                self.dma("sync", self.ctb[256 + h:257 + h, TQ * qi:TQ * (qi + 1)],
                         dstg[opar][:], inc=sm["ost"])
                ost_hist.append(sm["ost"].v)
                qsite += 1
                if qsite % 2 == 0:
                    emit_fold_slice()
            qk_after_head.append(sm["qk"].v)
        while self.krep_jobs:
            emit_krep_job()
        while fold_state["sl"] < 32:
            emit_fold_slice()
        self.att_ost = sm["ost"].v
        self.att_oden = sm["oden"].v
        self.fold_done = sm["foldev"].v
        self.krep_final = sm["krep"].v

        if "ctb" in self.dumps:
            tmp = ea(nc.sbuf_tensor("ctbtmp", [128, T], BF16))
            self.dump_dram("ctb", self.ctb, [264, T], BF16, tmp)

    def p3_collective(self):
        sm = self.sm
        self.emit("gpsimd",
                  lambda e: e.collective_compute(
                      "AllGather", OP.bypass,
                      ins=[self.ctb.opt()], outs=[self.gth.opt()],
                      replica_groups=RG),
                  wait=[(sm["ost"], self.att_ost)], inc=sm["cc"], incn=1)
        self.cc_done = sm["cc"].v

    def p4_w2_ln2(self):
        nc, sm = self.nc, self.sm
        self.es_att.close()
        self.es_kv.close()
        e4 = self.es_p4.enter_context
        orhs = e4(nc.sbuf_tensor("orhs", [128, 2 * 4096], BF16))
        deng = e4(nc.sbuf_tensor("deng", [32, 1024], BF16))
        rdexp = e4(nc.sbuf_tensor("rdexp", [32, 1024], F32))
        rdb = [e4(nc.sbuf_tensor(f"rdb{i}", [128, TQ], F32)) for i in range(2)]
        x1sl = e4(nc.sbuf_tensor("x1sl", [128, 32 * TQ], F32))
        x1b = [e4(nc.sbuf_tensor(f"x1b{i}", [128, TQ], BF16)) for i in range(2)]
        x1q = [e4(nc.sbuf_tensor(f"x1q{i}", [128, TQ], BF16)) for i in range(2)]
        xqt = [e4(nc.sbuf_tensor(f"xqt{i}", [128, TQ], F32)) for i in range(2)]
        w2tmp = [e4(nc.sbuf_tensor(f"w2tmp{i}", [128, TQ], F32)) for i in range(2)]
        h2stg = [e4(nc.sbuf_tensor(f"h2stg{i}", [128, TQ], BF16)) for i in range(2)]
        h2w = [e4(nc.sbuf_tensor(f"h2w{i}", [128, TQ], F32)) for i in range(2)]
        m2bc = e4(nc.sbuf_tensor("m2bc", [128, TQ], F32))
        r2bc = e4(nc.sbuf_tensor("r2bc", [128, TQ], F32))

        cc_h, cc_v = sm["cc"].h, self.cc_done
        grd_h = sm["grd"].h
        gth, xqin = self.gth, self.xqi

        use_switch = self.use_switch

        def emit_grd(e, g):
            c0 = 1024 * g
            for tt in range(2):
                for jjc in range(8):
                    row = 264 * (jjc // 2) + 128 * (jjc % 2)
                    e.dma_start(out=orhs[:, 4096 * tt + 512 * jjc:4096 * tt + 512 * (jjc + 1)],
                                in_=gth[row:row + 128, c0 + TQ * tt:c0 + TQ * (tt + 1)]
                                ).then_inc(grd_h, 16)
            for r in range(4):
                e.dma_start(out=deng[8 * r:8 * (r + 1), :],
                            in_=gth[264 * r + 256:264 * r + 264, c0:c0 + 1024]
                            ).then_inc(grd_h, 16)

        def grd_block(e):
            e.wait_ge(cc_h, cc_v)
            if use_switch:
                for arm in e.switch_core_id(n=128):
                    emit_grd(e, arm.logical % 4)
            else:
                emit_grd(e, 0)
        self.plan["sync"].append(grd_block)
        sm["grd"].v += 16 * 20
        grd_done = sm["grd"].v

        self.emit("scalar", lambda e: e.activation(
            out=rdexp[:], in_=deng[:], func=AF.Ln, bias=0.0, scale=1.0),
            wait=[(sm["grd"], grd_done)])
        self.emit("scalar", lambda e: e.activation(
            out=deng[:], in_=rdexp[:], func=AF.Exp, bias=0.0, scale=-1.0), inc=sm["rden"])
        rden_done = sm["rden"].v

        onrm_hist = []
        nsite = 0
        for tt in range(2):
            for jjc in range(8):
                par = nsite % 2
                w = [(sm["rden"], rden_done)]
                if nsite == 0:
                    w.append((sm["krep"], self.krep_final))
                else:
                    w.append((sm["onrm"], 2 * nsite - 1))
                self.emit("tensor",
                          lambda e, jjc=jjc, tt=tt: e.matmul(
                              self.psx1[:, 0:512], self.indsb[:, 128 * jjc:128 * (jjc + 1)],
                              deng[:, TQ * tt:TQ * (tt + 1)], start=True, stop=True),
                          wait=w, inc=sm["indm"])
                evw = [(sm["indm"], sm["indm"].v)]
                if nsite >= 2:
                    evw.append((sm["onrm"], onrm_hist[nsite - 2]))
                self.emit("vector",
                          lambda e, par=par: e.tensor_copy(rdb[par][:], self.psx1[:, 0:512]),
                          wait=evw, inc=sm["onrm"])
                sl = slice(4096 * tt + 512 * jjc, 4096 * tt + 512 * (jjc + 1))
                self.emit("vector",
                          lambda e, sl=sl, par=par: e.tensor_tensor(
                              orhs[:, sl], orhs[:, sl], rdb[par][:], OP.mult),
                          wait=[(sm["grd"], grd_done)], inc=sm["onrm"])
                onrm_hist.append(sm["onrm"].v)
                nsite += 1
        orhs_done = sm["onrm"].v

        wsite = 0
        w2ep_a_hist = []
        w2ep_d_hist = []
        l2st_hist = []
        xq_hist = []
        h2st_hist = []
        for tt in range(2):
            for oc in range(32):
                bank = self.bank4[wsite % 4]
                w = [(sm["onrm"], orhs_done)]
                if wsite == 0:
                    w.append((sm["foldev"], self.fold_done))
                if tt == 1 and oc == 0:
                    w.append((sm["h2st"], sm["h2st"].v))
                    w.append((sm["x1st"], sm["x1st"].v))
                if wsite >= 4:
                    w.append((sm["w2ep_a"], w2ep_a_hist[wsite - 4]))
                for jjc in range(8):
                    self.emit("tensor",
                              lambda e, bank=bank, jjc=jjc, oc=oc, tt=tt: e.matmul(
                                  bank,
                                  self.w2ft[:, 4096 * jjc + 128 * oc:4096 * jjc + 128 * (oc + 1)],
                                  orhs[:, 4096 * tt + 512 * jjc:4096 * tt + 512 * (jjc + 1)],
                                  start=(jjc == 0), stop=(jjc == 7)),
                              wait=w if jjc == 0 else [],
                              inc=(sm["w2mm"] if jjc == 7 else None))
                mm_done = sm["w2mm"].v
                par = wsite % 2
                xw = []
                if wsite >= 2:
                    xw.append((sm["w2ep_d"], w2ep_d_hist[wsite - 2]))
                if wsite == 0 and tt == 0:
                    pass
                self.dma("sync", xqt[par][:],
                         xqin[128 * oc:128 * (oc + 1), 512 * tt:512 * (tt + 1)],
                         wait=xw, inc=sm["xqld"])
                xq_hist.append(sm["xqld"].v)
                aw = [(sm["w2mm"], mm_done)]
                if wsite >= 2:
                    aw.append((sm["w2ep_d"], w2ep_d_hist[wsite - 2]))
                self.emit("scalar",
                          lambda e, bank=bank, par=par, oc=oc: e.activation(
                              out=w2tmp[par][:], in_=bank, func=AF.Identity,
                              bias=self.b2c[:, oc:oc + 1], scale=1.0),
                          wait=aw, inc=sm["w2ep_a"])
                w2ep_a_hist.append(sm["w2ep_a"].v)
                dw = [(sm["w2ep_a"], sm["w2ep_a"].v), (sm["xqld"], xq_hist[-1])]
                if tt == 1 and oc == 0:
                    dw.append((sm["x1st"], 16))
                self.emit("vector",
                          lambda e, par=par, oc=oc: e.tensor_tensor(
                              x1sl[:, TQ * oc:TQ * (oc + 1)], w2tmp[par][:], xqt[par][:],
                              OP.add),
                          wait=dw, inc=sm["w2ep_d"])
                w2ep_d_hist.append(sm["w2ep_d"].v)
                x1_done = sm["w2ep_d"].v
                sqw = [(sm["w2ep_d"], x1_done)]
                if wsite >= 2:
                    sqw.append((sm["l2st"], l2st_hist[wsite - 2]))
                self.emit("scalar",
                          lambda e, par=par, oc=oc: e.activation(
                              out=x1q[par][:], in_=x1sl[:, TQ * oc:TQ * (oc + 1)],
                              func=AF.Square, bias=0.0, scale=1.0),
                          wait=sqw, inc=sm["w2ep_a"])
                self.emit("vector",
                          lambda e, par=par, oc=oc: e.tensor_copy(
                              x1b[par][:], x1sl[:, TQ * oc:TQ * (oc + 1)]),
                          wait=sqw, inc=sm["w2ep_d"])
                pw = [(sm["w2ep_d"], sm["w2ep_d"].v), (sm["w2ep_a"], sm["w2ep_a"].v)]
                if wsite == 0:
                    pw.append((sm["oden"], self.att_oden))
                self.emit("tensor",
                          lambda e, par=par, oc=oc: e.matmul(
                              self.psav[0:1, 0:512], self.onescol[:], x1b[par][:],
                              start=(oc == 0), stop=(oc == 31)),
                          wait=pw)
                self.emit("tensor",
                          lambda e, par=par, oc=oc: e.matmul(
                              self.psx2[0:1, 0:512], self.onescol[:], x1q[par][:],
                              start=(oc == 0), stop=(oc == 31)),
                          inc=sm["l2st"])
                l2st_hist.append(sm["l2st"].v)
                wsite += 1
            stats_done = sm["l2st"].v
            self.emit("vector", lambda e: e.tensor_scalar(
                self.row_a[:], self.psav[0:1, 0:512], -1.0 / D, None, OP.mult),
                wait=[(sm["l2st"], stats_done)])
            self.emit("vector", lambda e: e.tensor_tensor(
                self.row_b[:], self.row_a[:], self.row_a[:], OP.mult))
            self.emit("vector", lambda e: e.tensor_scalar(
                rdexp[0:1, 0:512], self.psx2[0:1, 0:512], 1.0 / D, EPS, OP.mult, OP.add))
            self.emit("vector", lambda e: e.tensor_tensor(
                rdexp[0:1, 0:512], rdexp[0:1, 0:512], self.row_b[:], OP.subtract),
                inc=sm["l2rows"])
            self.emit("scalar", lambda e: e.activation(
                out=rdexp[0:1, 0:512], in_=rdexp[0:1, 0:512], func=AF.Ln, bias=0.0, scale=1.0),
                wait=[(sm["l2rows"], sm["l2rows"].v)])
            self.emit("scalar", lambda e: e.activation(
                out=self.row_b[:], in_=rdexp[0:1, 0:512], func=AF.Exp, bias=0.0, scale=-0.5),
                inc=sm["l2rows"])
            rr = sm["l2rows"].v
            self.emit("vector", lambda e: e.tensor_copy(self.row_c[:], self.row_a[:]),
                      wait=[(sm["l2rows"], rr)])
            self.emit("vector", lambda e: e.tensor_copy(self.row_d[:], self.row_b[:]),
                      inc=sm["l2rows"])
            rows_bf = sm["l2rows"].v
            self.emit("tensor", lambda e: e.matmul(
                self.psx2[:, 0:512], self.onesrow[:], self.row_c[:], start=True, stop=True),
                wait=[(sm["l2rows"], rows_bf), (sm["l2bc"], sm["l2bc"].v)], inc=sm["indm"])
            self.emit("vector", lambda e: e.tensor_copy(m2bc[:], self.psx2[:, 0:512]),
                      wait=[(sm["indm"], sm["indm"].v)], inc=sm["l2bc"])
            self.emit("tensor", lambda e: e.matmul(
                self.psx2[:, 0:512], self.onesrow[:], self.row_d[:], start=True, stop=True),
                wait=[(sm["l2bc"], sm["l2bc"].v)], inc=sm["indm"])
            self.emit("vector", lambda e: e.tensor_copy(r2bc[:], self.psx2[:, 0:512]),
                      wait=[(sm["indm"], sm["indm"].v)], inc=sm["l2bc"])
            bc_done = sm["l2bc"].v
            for oc in range(32):
                par = oc % 2
                w = [(sm["l2bc"], bc_done)]
                if len(h2st_hist) >= 2:
                    w.append((sm["h2st"], h2st_hist[-2]))
                self.emit("vector",
                          lambda e, oc=oc, par=par: e.tensor_tensor(
                              h2w[par][:], x1sl[:, TQ * oc:TQ * (oc + 1)], m2bc[:], OP.add),
                          wait=w)
                self.emit("vector",
                          lambda e, oc=oc, par=par: e.tensor_tensor(
                              h2stg[par][:], h2w[par][:], r2bc[:], OP.mult),
                          inc=sm["h2dv"])
                self.dma("sync", self.h2d[128 * oc:128 * (oc + 1), 512 * tt:512 * (tt + 1)],
                         h2stg[par][:], wait=[(sm["h2dv"], sm["h2dv"].v)], inc=sm["h2st"])
                h2st_hist.append(sm["h2st"].v)
            self.dma("sync",
                     self.x1d[:, 512 * tt:512 * (tt + 1)].rearrange("(c p) f -> p c f", p=128),
                     x1sl[:].rearrange("p (c f) -> p c f", f=TQ),
                     wait=[(sm["h2dv"], sm["h2dv"].v)], inc=sm["x1st"])
        self.h2_stored = sm["h2st"].v
        self.x1_stored = sm["x1st"].v

        if "x1" in self.dumps:
            tmp = e4(nc.sbuf_tensor("x1tmp", [128, 1024], F32))
            self.dump_dram("x1", self.x1d, [D, 1024], F32, tmp)
        if "h2" in self.dumps:
            tmp2 = e4(nc.sbuf_tensor("h2tmp", [128, 1024], BF16))
            self.dump_dram("h2", self.h2d, [D, 1024], BF16, tmp2)

    def p5_ffn(self):
        nc, sm = self.nc, self.sm
        self.es_p4.close()
        self.es_w2.close()
        e5 = self.es.enter_context
        h2sb = e5(nc.sbuf_tensor("h2sb", [128, 32 * 1024], BF16))
        hff = e5(nc.sbuf_tensor("hff", [128, 29 * 1024], BF16))
        wg = [e5(nc.sbuf_tensor(f"wg{i}", [128, 32 * 128], BF16)) for i in range(2)]
        wu = [e5(nc.sbuf_tensor(f"wu{i}", [128, 32 * 128], BF16)) for i in range(2)]
        gst = [e5(nc.sbuf_tensor(f"gst{i}", [128, TQ], BF16)) for i in range(2)]
        wd = [e5(nc.sbuf_tensor(f"wd{i}", [128, 29 * 128], BF16)) for i in range(2)]
        ypst = [e5(nc.sbuf_tensor(f"ypst{i}", [128, TQ], F32)) for i in range(2)]
        ypldb = [e5(nc.sbuf_tensor(f"ypld{i}", [128, TQ], F32)) for i in range(2)]
        x1ldb = [e5(nc.sbuf_tensor(f"x1ld{i}", [128, TQ], F32)) for i in range(2)]

        for q4 in range(4):
            self.dma("sync",
                     h2sb[:].rearrange("p (c f) -> p c f", f=1024)[:, 8 * q4:8 * (q4 + 1), :],
                     self.h2d[1024 * q4:1024 * (q4 + 1), :].rearrange("(c p) f -> p c f", p=128),
                     wait=[(sm["h2st"], self.h2_stored), (sm["x1st"], self.x1_stored)]
                     if q4 == 0 else [], inc=sm["h2ld"])
        h2_ready = sm["h2ld"].v

        gumm_hist = []
        hffm_hist = []
        silu_hist = []
        dnmm_hist = []
        dnev_hist = []
        outst_hist = []
        ypd_store = {}
        gsite = 0
        dsite = 0
        hc0 = 0
        for third, nh in enumerate(THIRDS):
            dn_prev_done = sm["dnev"].v
            for hl in range(nh):
                hc = hc0 + hl
                par = gsite % 2
                gw_sem = sm["gwld0"] if par == 0 else sm["gwld1"]
                lw = []
                if gsite >= 2:
                    lw.append((sm["gumm"], gumm_hist[gsite - 2]))
                if gsite == 0:
                    lw.append((sm["w2mm"], sm["w2mm"].v))
                self.dma("gpsimd", wg[par][:].rearrange("p (c o) -> p c o", o=128),
                         self.wgt[:, 128 * hc:128 * (hc + 1)].rearrange("(c p) o -> p c o", p=128),
                         wait=lw, inc=gw_sem)
                self.dma("gpsimd", wu[par][:].rearrange("p (c o) -> p c o", o=128),
                         self.wut[:, 128 * hc:128 * (hc + 1)].rearrange("(c p) o -> p c o", p=128),
                         inc=gw_sem)
                w_ready = gw_sem.v
                for tt in range(2):
                    pidx = 2 * gsite + tt
                    gbank = self.psA[:, 512 * tt:512 * (tt + 1)]
                    ubank = self.psB[:, 512 * tt:512 * (tt + 1)]
                    mmw = [(gw_sem, w_ready), (sm["h2ld"], h2_ready)]
                    if hl == 0 and third > 0:
                        mmw.append((sm["dnev"], dn_prev_done))
                    if pidx >= 2:
                        mmw.append((sm["hffm"], hffm_hist[pidx - 2]))
                    for c in range(32):
                        self.emit("tensor",
                                  lambda e, c=c, par=par, tt=tt, gbank=gbank: e.matmul(
                                      gbank, wg[par][:, 128 * c:128 * (c + 1)],
                                      h2sb[:, 1024 * c + 512 * tt:1024 * c + 512 * (tt + 1)],
                                      start=(c == 0), stop=(c == 31)),
                                  wait=mmw if c == 0 else [])
                    for c in range(32):
                        self.emit("tensor",
                                  lambda e, c=c, par=par, tt=tt, ubank=ubank: e.matmul(
                                      ubank, wu[par][:, 128 * c:128 * (c + 1)],
                                      h2sb[:, 1024 * c + 512 * tt:1024 * c + 512 * (tt + 1)],
                                      start=(c == 0), stop=(c == 31)),
                                  inc=(sm["gumm"] if c == 31 else None))
                    mm_done = sm["gumm"].v
                    spar = pidx % 2
                    aw = [(sm["gumm"], mm_done)]
                    if pidx >= 2:
                        aw.append((sm["hffm"], hffm_hist[pidx - 2]))
                    self.emit("scalar",
                              lambda e, gbank=gbank, spar=spar: e.activation(
                                  out=gst[spar][:], in_=gbank, func=AF.Silu,
                                  bias=0.0, scale=1.0),
                              wait=aw, inc=sm["silu"])
                    silu_hist.append(sm["silu"].v)
                    self.emit("vector",
                              lambda e, ubank=ubank, spar=spar, hl=hl, tt=tt: e.tensor_tensor(
                                  hff[:, 1024 * hl + 512 * tt:1024 * hl + 512 * (tt + 1)],
                                  ubank, gst[spar][:], OP.mult),
                              wait=[(sm["silu"], sm["silu"].v)], inc=sm["hffm"])
                    hffm_hist.append(sm["hffm"].v)
                gumm_hist.append(sm["gumm"].v)
                gsite += 1
            hff_done = sm["hffm"].v

            for oc in range(32):
                par = dsite % 2
                wd_sem = sm["wdld0"] if par == 0 else sm["wdld1"]
                lw = []
                if oc == 0:
                    lw.append((sm["hffm"], hff_done))
                if dsite >= 2:
                    lw.append((sm["dnmm"], dnmm_hist[2 * (dsite - 2) + 1]))
                self.dma("gpsimd",
                         wd[par][:, 0:nh * 128].rearrange("p (c o) -> p c o", o=128),
                         self.wdt[128 * hc0:128 * (hc0 + nh), 128 * oc:128 * (oc + 1)]
                         .rearrange("(c p) o -> p c o", p=128),
                         wait=lw, inc=wd_sem)
                wd_ready = wd_sem.v
                for tt in range(2):
                    pidx = 2 * dsite + tt
                    bank = self.bank4[pidx % 4]
                    mmw = [(wd_sem, wd_ready), (sm["hffm"], hff_done)]
                    if len(dnev_hist) >= 4:
                        mmw.append((sm["dnev"], dnev_hist[pidx - 4]))
                    for c in range(nh):
                        self.emit("tensor",
                                  lambda e, c=c, par=par, tt=tt, bank=bank, nh=nh: e.matmul(
                                      bank, wd[par][:, 128 * c:128 * (c + 1)],
                                      hff[:, 1024 * c + 512 * tt:1024 * c + 512 * (tt + 1)],
                                      start=(c == 0), stop=(c == nh - 1)),
                                  wait=mmw if c == 0 else [],
                                  inc=(sm["dnmm"] if c == nh - 1 else None))
                    dnmm_hist.append(sm["dnmm"].v)
                    mm_done = sm["dnmm"].v
                    spar = pidx % 2
                    col = slice(512 * tt, 512 * (tt + 1))
                    if third == 0:
                        cw = [(sm["dnmm"], mm_done)]
                        if len(outst_hist) >= 2:
                            cw.append((sm["outst"], outst_hist[-2]))
                        self.emit("vector",
                                  lambda e, bank=bank, spar=spar: e.tensor_copy(
                                      ypst[spar][:], bank),
                                  wait=cw, inc=sm["dnev"])
                        self.dma("sync", self.ypd[128 * oc:128 * (oc + 1), col], ypst[spar][:],
                                 wait=[(sm["dnev"], sm["dnev"].v)], inc=sm["outst"])
                    else:
                        ldw = [(sm["outst"], ypd_store[(oc, tt)])]
                        if len(dnev_hist) >= 2:
                            ldw.append((sm["dnev"], dnev_hist[pidx - 2]))
                        self.dma("sync", ypldb[spar][:], self.ypd[128 * oc:128 * (oc + 1), col],
                                 wait=ldw, inc=sm["ypld"])
                        yp_ready = sm["ypld"].v
                        if third == 1:
                            cw = [(sm["dnmm"], mm_done), (sm["ypld"], yp_ready)]
                            if len(outst_hist) >= 2:
                                cw.append((sm["outst"], outst_hist[-2]))
                            self.emit("vector",
                                      lambda e, bank=bank, spar=spar: e.tensor_tensor(
                                          ypst[spar][:], bank, ypldb[spar][:], OP.add),
                                      wait=cw, inc=sm["dnev"])
                            self.dma("sync", self.ypd[128 * oc:128 * (oc + 1), col],
                                     ypst[spar][:], wait=[(sm["dnev"], sm["dnev"].v)],
                                     inc=sm["outst"])
                        else:
                            xlw = [(sm["x1st"], self.x1_stored)]
                            if len(dnev_hist) >= 2:
                                xlw.append((sm["dnev"], dnev_hist[pidx - 2]))
                            self.dma("sync", x1ldb[spar][:],
                                     self.x1d[128 * oc:128 * (oc + 1), col],
                                     wait=xlw, inc=sm["x1ld"])
                            cw = [(sm["dnmm"], mm_done), (sm["ypld"], yp_ready),
                                  (sm["x1ld"], sm["x1ld"].v)]
                            if len(outst_hist) >= 2:
                                cw.append((sm["outst"], outst_hist[-2]))
                            self.emit("vector",
                                      lambda e, bank=bank, spar=spar: e.tensor_tensor(
                                          ypst[spar][:], bank, ypldb[spar][:], OP.add),
                                      wait=cw)
                            self.emit("vector",
                                      lambda e, spar=spar: e.tensor_tensor(
                                          ypst[spar][:], ypst[spar][:], x1ldb[spar][:], OP.add),
                                      inc=sm["dnev"])
                            self.dma("sync", self.out[128 * oc:128 * (oc + 1), col],
                                     ypst[spar][:], wait=[(sm["dnev"], sm["dnev"].v)],
                                     inc=sm["outst"])
                    ypd_store[(oc, tt)] = sm["outst"].v
                    dnev_hist.append(sm["dnev"].v)
                    outst_hist.append(sm["outst"].v)
                dsite += 1
            hc0 += nh



def make_consts():
    E = np.zeros((4, 128, 128), dtype=np.float32)
    for hp in range(4):
        for d in range(128):
            E[hp, 32 * hp + d // 4, d] = 1.0
    Epk = np.concatenate([E[i] for i in range(4)], axis=1)
    F = np.zeros((4, 128, 128), dtype=np.float32)
    for i in range(4):
        for c in range(128):
            F[i, c, 32 * i + c // 4] = 1.0
    Fpk = np.concatenate([F[i] for i in range(4)], axis=1)
    I = np.eye(128, dtype=np.float32)
    M = np.zeros((128, 2048), dtype=np.float32)
    for r in range(4):
        for p in range(128):
            lo = 512 * r + min(128 * r + p, 512)
            M[p, lo:512 * (r + 1)] = 1.0
    IND = np.zeros((32, 1024), dtype=np.float32)
    for c in range(8):
        for j in range(128):
            IND[4 * c + j // 32, 128 * c + j] = 1.0
    toBF = lambda a: np.ascontiguousarray(a.astype(BF))
    return dict(cE=toBF(Epk), cF=toBF(Fpk), cI=toBF(I), cM=toBF(M),
                cOc=toBF(np.ones((128, 1), np.float32)),
                cOr=toBF(np.ones((1, 128), np.float32)), cIND=toBF(IND))


def stage_inputs(inputs):
    x = np.asarray(inputs["L0_attn_x"], dtype=np.float32)
    W1 = np.asarray(inputs["L0_attn_W1"], dtype=np.float32)
    b1 = np.asarray(inputs["L0_attn_b1"], dtype=np.float32)
    W2 = np.asarray(inputs["L0_attn_W2"], dtype=np.float32)
    b2 = np.asarray(inputs["L0_attn_b2"], dtype=np.float32)
    Wg = np.asarray(inputs["L0_ffn_W_gate"], dtype=np.float32)
    Wu = np.asarray(inputs["L0_ffn_W_up"], dtype=np.float32)
    Wd = np.asarray(inputs["L0_ffn_W_down"], dtype=np.float32)
    consts = make_consts()
    xT = [np.ascontiguousarray(x[b].T) for b in range(B)]
    w2t = np.ascontiguousarray(W2.T)
    wgt = np.ascontiguousarray(Wg.T)
    wut = np.ascontiguousarray(Wu.T)
    wdt = np.ascontiguousarray(Wd.T)
    w1T = W1.T
    in_maps = []
    for cid in range(NCORE):
        b, g = cid // 4, cid % 4
        cols = np.r_[1024 * g:1024 * (g + 1),
                     4096 + 256 * g:4096 + 256 * (g + 1),
                     5120 + 256 * g:5120 + 256 * (g + 1)]
        im = dict(consts)
        im["xT"] = xT[b]
        im["xq"] = np.ascontiguousarray(xT[b][:, 1024 * g:1024 * (g + 1)])
        im["w1t"] = np.ascontiguousarray(w1T[:, cols])
        im["b1v"] = np.ascontiguousarray(b1[cols])
        im["w2t"] = w2t
        im["b2v"] = b2
        im["wgt"] = wgt
        im["wut"] = wut
        im["wdt"] = wdt
        in_maps.append(im)
    return in_maps


def run(inputs, phases=9, dumps=(), use_switch=True):
    in_maps = stage_inputs(inputs)
    bld = Builder(dumps=dumps, use_switch=use_switch)
    nc = bld.build(phases=phases)
    return run_bass_kernel_spmd(nc, in_maps, core_ids=list(range(NCORE)))


def kernel(**inputs):
    res = run(inputs, phases=9)
    full = np.empty((B, T, D), dtype=np.float32)
    for cid in range(NCORE):
        b, g = cid // 4, cid % 4
        full[b, 1024 * g:1024 * (g + 1), :] = np.asarray(res.results[cid]["out"]).T
    return full
